# version 05
# baseline (speedup 1.0000x reference)
import numpy as np
import ml_dtypes
import concourse.bass as bass
import concourse.mybir as mybir
from concourse.bass_utils import run_bass_kernel_spmd

F32 = mybir.dt.float32
BF16 = mybir.dt.bfloat16
AF = mybir.ActivationFunctionType
ALU = mybir.AluOpType
AX = mybir.AxisListType.X

T = 2048
D = 2048
NE = 16
HID = 8192
NH = 64
NHEAD = 32
HN = 64
Q = 256
NCH = 8
LN_EPS = 1e-5
GN_EPS = 1e-5 * 64
BIG_SIZE = 131200


class Sem:
    __slots__ = ("h", "n", "name")

    def __init__(self, nc, name):
        self.h = nc.alloc_semaphore(name)
        self.n = 0
        self.name = name


class KB:
    def __init__(self, nc):
        self.nc = nc
        self.sems = []
        self.cur = None
        self.eng = None

    def sem(self, name):
        s = Sem(self.nc, name)
        self.sems.append(s)
        return s

    def begin(self, cur, eng):
        self.cur = cur
        self.eng = eng
        for s in self.sems:
            s.n = 0

    def op(self, engname, method, *args, waits=(), incs=(), **kwargs):
        waits = [w for w in waits if w is not None]
        if self.cur == engname:
            for s, v in waits:
                w = s.n if v is None else v
                if w > 0:
                    self.eng.wait_ge(s.h, w)
            ins = getattr(self.eng, method)(*args, **kwargs)
            for s, a in incs:
                ins = ins.then_inc(s.h, a)
        for s, a in incs:
            s.n += a

    def wait(self, engname, sem, val=None):
        if self.cur == engname:
            w = sem.n if val is None else val
            if w > 0:
                self.eng.wait_ge(sem.h, w)

    def dma(self, engname, sem, *, out, in_, waits=(), **kw):
        self.op(engname, "dma_start", out=out, in_=in_, waits=waits,
                incs=[(sem, 16)], **kw)
        return sem.n


def build_nc(n_sub=4, dbg=None):
    nc = bass.Bass("TRN2", target_bir_lowering=False)

    X_IN = nc.dram_tensor("x_in", [D, T], F32, kind="ExternalInput").ap()
    Y_OUT = nc.dram_tensor("y_out", [D, T], F32, kind="ExternalOutput").ap()
    DBG = (nc.dram_tensor("dbg", [D, T], F32, kind="ExternalOutput").ap()
           if dbg else None)

    def din(name, shape, dt=F32):
        return nc.dram_tensor(name, shape, dt, kind="ExternalInput").ap()

    aW = {p: din("aW" + p, [D, D]) for p in "krvgo"}
    ab = {p: din("ab" + p, [128, 16]) for p in "krvgo"}
    bW = {p: din("bW" + p, [D, D]) for p in "krvo"}
    fWg = din("fWg", [D, HID])
    fWu = din("fWu", [D, HID])
    fWd = din("fWd", [HID, D])
    PRM = {}
    for nm in ("maa_k", "maa_v", "maa_r", "maa_g", "alng", "alnb",
               "tm_k", "tm_v", "tm_r", "blng", "blnb", "btf"):
        PRM[nm] = din(nm, [128, 16])
    atd_row = din("atd_row", [1, 32])
    atf_row = din("atf_row", [1, 32])
    C_ones = din("c_ones", [1, 512])
    C_onesb = din("c_onesb", [1, 512], BF16)
    C_iota = din("c_iota", [1, 256])
    C_revi = din("c_revi", [1, 256])
    C_iotab = din("c_iotab", [128, 256])
    C_dtile = din("c_dtile", [128, 512])
    C_step = din("c_step", [128, 512], BF16)
    C_diag = din("c_diag", [128, 512], BF16)
    C_bdiag = din("c_bdiag", [128, 512], BF16)
    C_sel = din("c_sel", [32, 2048], BF16)
    C_oneh4 = din("c_oneh4", [128, 16], BF16)

    def dscr(name, shape, dt):
        return nc.dram_tensor(name, shape, dt, kind="Internal").ap()

    xA = dscr("xA", [D, T], F32)
    xB = dscr("xB", [D, T], F32)
    spK = dscr("spK", [D, T], BF16)
    spR = dscr("spR", [D, T], BF16)
    spV = dscr("spV", [D, T], BF16)
    spG = dscr("spG", [D, T], BF16)
    spZ = dscr("spZ", [D, T], BF16)
    spF = dscr("spF", [HID, T], BF16)

    def sb(name, shape, dt):
        return nc.alloc_sbuf_tensor(name, shape, dt)

    _big_off = [None]

    def sba(name, shape, dt, off):
        if _big_off[0] is None:
            top = 229344
            _big_off[0] = (top - nc.sbuf_bytes_remaining + 63) & ~63
            assert _big_off[0] + BIG_SIZE <= top,                 f"SBUF overlay overflow: base {_big_off[0]}"
        return nc.alloc_sbuf_tensor_at(name, shape, dt,
                                       offset=_big_off[0] + off)

    ones_row = sb("ones_row", [1, 512], F32)
    ones_bf = sb("ones_bf", [1, 512], BF16)
    iota_row = sb("iota_row", [1, 256], F32)
    revi_row = sb("revi_row", [1, 256], F32)
    iota_b = sb("iota_b", [128, 256], F32)
    dtile = sb("dtile", [128, 512], F32)
    stepd = sb("stepd", [128, 512], BF16)
    diagd = sb("diagd", [128, 512], BF16)
    bdiag = sb("bdiag", [128, 512], BF16)
    sel = sb("sel", [32, 2048], BF16)
    oneh4 = sb("oneh4", [128, 16], BF16)
    prm = {nm: sb("p_" + nm, [128, 16], F32) for nm in PRM}
    bias = {p: sb("bias_" + p, [128, 16], F32) for p in "krvgo"}
    zbias = sb("zbias", [128, 16], F32)
    Mmax = sb("Mmax", [128, 16], F32)
    negM = sb("negM", [128, 16], F32)
    gu = sb("gu", [128, 16], F32)
    td_row = sb("td_row", [1, 32], F32)
    tf_row = sb("tf_row", [1, 32], F32)
    lnw_row = sb("lnw_row", [1, 32], F32)
    lnw_b = sb("lnw_b", [128, 32], F32)
    u_b = sb("u_b", [128, 32], F32)
    wkv_c0 = sb("wkv_c0", [128, 32], F32)
    wkv_c1 = sb("wkv_c1", [128, 32], F32)
    ws_b = sb("ws_b", [128, 32], F32)
    stat_a = sb("stat_a", [4, 512], F32)
    stat_b = sb("stat_b", [4, 512], F32)
    mu_row = sb("mu_row", [1, 2048], BF16)
    rstd_row = sb("rstd_row", [1, 2048], BF16)
    mub = sb("mub", [128, 2048], BF16)
    rstdb = sb("rstdb", [128, 2048], BF16)
    gnst = sb("gnst", [32, 1024], F32)
    gnrow_a = sb("gnrow_a", [32, 256], BF16)
    gnrow_m = sb("gnrow_m", [32, 256], BF16)
    wb_s = [sb("wb%d" % i, [128, 2048], BF16) for i in range(4)]
    evac0 = sb("evac0", [128, 2048], BF16)
    evac1 = sb("evac1", [128, 2048], BF16)
    xew = sb("xew", [128, 2048], F32)
    f32t = sb("f32t", [128, 512], F32)
    epsln = sb("epsln", [128, 1], F32)
    epsgn = sb("epsgn", [128, 1], F32)
    sqh = sb("sqh", [128, 1024], BF16)

    xln = sba("xln", [128, 16 * 2052], BF16, 0)
    xmix = sba("xmix", [128, 16 * 2048], BF16, 65664)
    wkvb = sba("wkvb", [128, 16 * 2048], BF16, 0)
    w4 = sba("w4", [128, 8192], F32, 65664)
    zhalf = sba("zhalf", [128, 64 * 1024], BF16, 0)
    kb_s = [sba("kbuf%d" % i, [128, 4096], BF16, i * 8192) for i in range(2)]
    rb_s = [sba("rbuf%d" % i, [128, 4096], BF16, 16384 + i * 8192)
            for i in range(2)]
    gb = sba("gbuf", [128, 4096], BF16, 32768)
    ktm_s = [sba("ktm%d" % i, [128, 4096], BF16, 40960 + i * 8192)
             for i in range(2)]
    vtm_s = [sba("vtm0", [128, 4096], BF16, 57344),
             sba("vtm1", [128, 4096], BF16, 121856)]
    masks = sba("masks", [128, 32 * 512], BF16, 65536)
    wbpk = sba("wbpk", [128, 16 * 256], BF16, 98304)
    stt_f = sba("stt_f", [128, 1024], F32, 106496)
    stt_b = sba("stt_b", [128, 1024], BF16, 110592)
    attm = sba("attm", [128, 1024], BF16, 118784)
    rtil = sba("rtil", [128, 256], BF16, 120832)
    ktil = sba("ktil", [128, 256], BF16, 121344)

    ps = [nc.alloc_psum_tensor("ps%d" % i, [128, 512], F32) for i in range(8)]

    b = KB(nc)
    tM = b.sem("tM")
    vD = b.sem("vD")
    sA = b.sem("sA")
    eV = b.sem("eV")
    eA = b.sem("eA")
    eY = b.sem("eY")
    eS = b.sem("eS")
    sC = b.sem("sC")
    sMR = b.sem("sMR")
    sW = [b.sem("sW%d" % i) for i in range(4)]
    sE = [b.sem("sE%d" % i) for i in range(2)]
    sKB = [b.sem("sKB%d" % i) for i in range(2)]
    sRB = [b.sem("sRB%d" % i) for i in range(2)]
    sKT = [b.sem("sKT%d" % i) for i in range(2)]
    sVT = [b.sem("sVT%d" % i) for i in range(2)]
    sGB = b.sem("sGB")
    sZH = b.sem("sZH")
    sX = [b.sem("sX%d" % i) for i in range(2)]
    sSP = [b.sem("sSP%d" % i) for i in range(2)]
    sXA = [b.sem("sXA%d" % i) for i in range(2)]
    sXB = [b.sem("sXB%d" % i) for i in range(2)]
    sXO = [b.sem("sXO%d" % i) for i in range(2)]
    sDB = b.sem("sDB")
    XSEM = {1: sXA, 2: sXB}
    OSEM = {}

    xr_in = X_IN.rearrange("(c p) t -> c p t", p=128)
    xr_A = xA.rearrange("(c p) t -> c p t", p=128)
    xr_B = xB.rearrange("(c p) t -> c p t", p=128)
    xr_O = Y_OUT.rearrange("(c p) t -> c p t", p=128)
    XSRC = {0: xr_in, 1: xr_A, 2: xr_B}
    OSEM[id(xr_A)] = sXA
    OSEM[id(xr_B)] = sXB
    OSEM[id(xr_O)] = sXO

    def emit_all():
        for dst, src in ((ones_row, C_ones), (ones_bf, C_onesb),
                         (iota_row, C_iota), (revi_row, C_revi),
                         (iota_b, C_iotab), (dtile, C_dtile),
                         (stepd, C_step), (diagd, C_diag), (bdiag, C_bdiag),
                         (sel, C_sel), (oneh4, C_oneh4), (td_row, atd_row),
                         (tf_row, atf_row)):
            b.dma("sync", sC, out=dst[:, :], in_=src)
        for nm in PRM:
            b.dma("sync", sC, out=prm[nm][:, :], in_=PRM[nm])
        for p in "krvgo":
            b.dma("sync", sC, out=bias[p][:, :], in_=ab[p])
        s0 = sC.n
        b.wait("scalar", sC, s0)
        b.op("scalar", "activation", lnw_row[:, :], td_row[:, :], AF.Exp,
             incs=[(sA, 1)])
        b.op("vector", "tensor_scalar_mul", lnw_row[:, :], lnw_row[:, :],
             -1.0, waits=[(sA, None)], incs=[(vD, 1)])
        b.op("tensor", "matmul", ps[0][0:128, 0:32], ones_row[:, 0:128],
             lnw_row[:, 0:32], start=True, stop=True, waits=[(vD, None)],
             incs=[(tM, 1)])
        b.op("vector", "tensor_copy", lnw_b[:, :], ps[0][0:128, 0:32],
             waits=[(tM, None)], incs=[(vD, 1)])
        b.wait("tensor", sC, s0)
        b.op("tensor", "matmul", ps[1][0:128, 0:32], ones_row[:, 0:128],
             tf_row[:, :], start=True, stop=True, incs=[(tM, 1)])
        b.op("vector", "tensor_copy", u_b[:, :], ps[1][0:128, 0:32],
             waits=[(tM, None)], incs=[(vD, 1)])
        b.op("tensor", "matmul", ps[2][0:128, 0:32], revi_row[:, 0:128],
             lnw_row[:, 0:32], start=True, stop=True, waits=[(vD, None)],
             incs=[(tM, 1)])
        b.op("scalar", "activation", wkv_c0[:, :], ps[2][0:128, 0:32],
             AF.Exp, waits=[(tM, None)], incs=[(sA, 1)])
        b.op("tensor", "matmul", ps[3][0:128, 0:32], revi_row[:, 128:256],
             lnw_row[:, 0:32], start=True, stop=True, incs=[(tM, 1)])
        b.op("scalar", "activation", wkv_c1[:, :], ps[3][0:128, 0:32],
             AF.Exp, waits=[(tM, None)], incs=[(sA, 1)])
        b.op("vector", "tensor_scalar_mul", ws_b[:, :], lnw_b[:, :], 256.0,
             waits=[(sA, None)], incs=[(vD, 1)])
        b.op("scalar", "activation", ws_b[:, :], ws_b[:, :], AF.Exp,
             waits=[(vD, None)], incs=[(sA, 1)])
        b.op("scalar", "activation", gu[:, :], prm["btf"][:, :], AF.Exp,
             incs=[(sA, 1)])
        b.op("vector", "tensor_scalar_add", gu[:, :], gu[:, :], -1.0,
             waits=[(sA, None)], incs=[(vD, 1)])
        b.op("vector", "memset", Mmax[:, :], -1e30, incs=[(vD, 1)])
        b.op("vector", "memset", epsln[:, :], LN_EPS, incs=[(vD, 1)])
        b.op("vector", "memset", epsgn[:, :], GN_EPS, incs=[(vD, 1)])
        b.op("vector", "memset", zbias[:, :], 0.0, incs=[(vD, 1)])
        for ec in range(NE):
            b.op("vector", "memset", xln[:, ec * 2052:ec * 2052 + 1], 0.0,
                 incs=[(vD, 1)])

        def ln_pass(src):
            xs3 = XSRC[src]
            xsw = ([(XSEM[src][0], None), (XSEM[src][1], None)]
                   if src in XSEM else [])
            rec = {}
            for ec in range(2):
                rec[ec] = b.dma("gpsimd", sE[ec % 2],
                                out=(evac0 if ec % 2 == 0 else evac1)[:, :],
                                in_=xs3[ec],
                                waits=[(tM, None), (sA, None)] + xsw)
            for ec in range(NE):
                buf = evac0 if ec % 2 == 0 else evac1
                b.wait("tensor", sE[ec % 2], rec[ec])
                if ec == 0:
                    b.wait("tensor", vD, vD.n)
                for tc in range(4):
                    b.op("tensor", "matmul", ps[6][0:4, :],
                         oneh4[:, 4 * tc:4 * tc + 4],
                         buf[:, tc * 512:(tc + 1) * 512],
                         start=(ec == 0 and tc == 0), stop=False)
                for tc in range(4):
                    sq_t = sqh[:, (tc % 2) * 512:(tc % 2) * 512 + 512]
                    b.op("scalar", "activation", sq_t,
                         buf[:, tc * 512:(tc + 1) * 512], AF.Square,
                         waits=[(sE[ec % 2], rec[ec])
                                if tc == 0 else (tM, None)],
                         incs=[(sA, 1)])
                    b.wait("tensor", sA, sA.n)
                    b.op("tensor", "matmul", ps[7][0:4, :],
                         oneh4[:, 4 * tc:4 * tc + 4], sq_t,
                         start=(ec == 0 and tc == 0),
                         stop=(ec == NE - 1 and tc == 3),
                         incs=[(tM, 1)])
                if ec + 2 < NE:
                    rec[ec + 2] = b.dma("gpsimd", sE[ec % 2], out=buf[:, :],
                                        in_=xs3[ec + 2],
                                        waits=[(tM, None), (sA, None)])
            ln_finalize()

        def ln_finalize():
            b.op("vector", "tensor_scalar_mul", stat_a[:, :], ps[6][0:4, :],
                 1.0 / D, waits=[(tM, None)], incs=[(vD, 1)])
            b.op("vector", "tensor_scalar_mul", stat_b[:, :], ps[7][0:4, :],
                 1.0 / D, incs=[(vD, 1)])
            b.op("vector", "tensor_tensor", xew[0:4, 0:512], stat_a[:, :],
                 stat_a[:, :], ALU.mult, incs=[(vD, 1)])
            b.op("vector", "tensor_sub", stat_b[:, :], stat_b[:, :],
                 xew[0:4, 0:512], incs=[(vD, 1)])
            b.op("scalar", "activation", stat_b[:, :], stat_b[:, :], AF.Sqrt,
                 bias=epsln[0:4, :], waits=[(vD, None)], incs=[(sA, 1)])
            b.op("vector", "reciprocal", stat_b[:, :], stat_b[:, :],
                 waits=[(sA, None)], incs=[(vD, 1)])
            b.op("vector", "tensor_copy", sqh[0:4, 0:512], stat_a[:, :],
                 waits=[(sA, None)], incs=[(vD, 1)])
            b.op("vector", "tensor_copy", sqh[0:4, 512:1024], stat_b[:, :],
                 incs=[(vD, 1)])
            b.dma("gpsimd", sMR, out=mu_row[:, :],
                  in_=sqh[0:4, 0:512], waits=[(vD, None)])
            b.dma("gpsimd", sMR, out=rstd_row[:, :],
                  in_=sqh[0:4, 512:1024])
            for row, dst in ((mu_row, mub), (rstd_row, rstdb)):
                for tc in range(4):
                    bank = ps[4 + (tc % 2)]
                    b.wait("tensor", sMR, sMR.n)
                    b.wait("tensor", vD, vD.n)
                    b.op("tensor", "matmul", bank[:, :], ones_bf[:, 0:128],
                         row[:, tc * 512:(tc + 1) * 512], start=True,
                         stop=True, incs=[(tM, 1)])
                    b.op("vector", "tensor_copy",
                         dst[:, tc * 512:(tc + 1) * 512], bank[:, :],
                         waits=[(tM, None)], incs=[(vD, 1)])

        def normalize_all(src):
            xs3 = XSRC[src]
            xsw = ([(XSEM[src][0], None), (XSEM[src][1], None)]
                   if src in XSEM else [])
            for ec in range(NE):
                b.op("vector", "memset", xln[:, ec * 2052:ec * 2052 + 1],
                     0.0, incs=[(vD, 1)])
                r0 = b.dma("sync", sX[0], out=xew[:, :], in_=xs3[ec],
                           waits=[(vD, None)] + (xsw if ec == 0 else []))
                b.wait("vector", sX[0], r0)
                b.op("vector", "tensor_sub", xew[:, :], xew[:, :], mub[:, :],
                     incs=[(vD, 1)])
                b.op("vector", "tensor_mul",
                     xln[:, ec * 2052 + 1:ec * 2052 + 2049], xew[:, :],
                     rstdb[:, :], incs=[(vD, 1)])

        def mix_all(prm_t, style):
            for ec in range(NE):
                base_t = xln[:, ec * 2052:(ec + 1) * 2052]
                cur = base_t[:, 1:2049]
                shf = base_t[:, 0:2048]
                dst = xmix[:, ec * 2048:(ec + 1) * 2048]
                if style == "a":
                    b.op("vector", "tensor_sub", dst, shf, cur,
                         waits=[(tM, None), (vD, None)] if ec == 0 else [],
                         incs=[(vD, 1)])
                    b.op("vector", "scalar_tensor_tensor", dst, dst,
                         prm_t[:, ec:ec + 1], cur, ALU.mult, ALU.add,
                         incs=[(vD, 1)])
                else:
                    b.op("vector", "tensor_sub", dst, cur, shf,
                         waits=[(tM, None), (vD, None)] if ec == 0 else [],
                         incs=[(vD, 1)])
                    b.op("vector", "scalar_tensor_tensor", dst, dst,
                         prm_t[:, ec:ec + 1], shf, ALU.mult, ALU.add,
                         incs=[(vD, 1)])

        def proj_fm(WT, bias_t, act, spill, track_max=False):
            wr = WT.rearrange("(dc p) e -> p dc e", p=128)
            sp3 = spill.rearrange("(ec p) t -> ec p t", p=128)
            base_eV = eV.n
            rec_w = {}
            rec_sp = {}
            for eo in range(NE):
                rec_w[eo] = b.dma(
                    "gpsimd", sW[eo % 2],
                    out=wb_s[eo % 2].rearrange("p (dc e) -> p dc e", e=128),
                    in_=wr[:, :, eo * 128:(eo + 1) * 128],
                    waits=[(eV, base_eV + 4 * (eo - 1))] if eo >= 2 else
                    ([(vD, None), (tM, None)] if eo == 0 else []))
            for eo in range(NE):
                wt = wb_s[eo % 2]
                ev_t = evac0 if eo % 2 == 0 else evac1
                b.wait("tensor", sW[eo % 2], rec_w[eo])
                if eo == 0:
                    b.wait("tensor", vD, vD.n)
                    b.wait("tensor", sA, sA.n)
                if eo >= 2:
                    b.wait("scalar", sSP[eo % 2], rec_sp[eo - 2])
                for tc in range(4):
                    bank = ps[tc]
                    b.wait("tensor", eV, max(0, eV.n - 3))
                    for dc in range(NE):
                        last = (dc == NE - 1)
                        b.op("tensor", "matmul", bank[:, :],
                             wt[:, dc * 128:(dc + 1) * 128],
                             xmix[:, dc * 2048 + tc * 512:
                                  dc * 2048 + (tc + 1) * 512],
                             start=(dc == 0), stop=last,
                             incs=[(tM, 1)] if last else [])
                    b.op("scalar", "activation",
                         ev_t[:, tc * 512:(tc + 1) * 512], bank[:, :], act,
                         bias=bias_t[:, eo:eo + 1], waits=[(tM, None)],
                         incs=[(eV, 1)])
                    if track_max:
                        b.op("vector", "tensor_reduce", xew[:, tc:tc + 1],
                             ev_t[:, tc * 512:(tc + 1) * 512], AX, ALU.max,
                             waits=[(eV, None)], incs=[(vD, 1)])
                if track_max:
                    b.op("vector", "tensor_reduce", xew[:, 4:5],
                         xew[:, 0:4], AX, ALU.max, incs=[(vD, 1)])
                    b.op("vector", "tensor_tensor", Mmax[:, eo:eo + 1],
                         Mmax[:, eo:eo + 1], xew[:, 4:5], ALU.max,
                         incs=[(vD, 1)])
                rec_sp[eo] = b.dma("sync", sSP[eo % 2], out=sp3[eo],
                                   in_=ev_t[:, :],
                                   waits=([(eV, None), (vD, None)]
                                          if track_max else [(eV, None)]))

        def residual_proj(WT, rhs_buf, rhs_w, src, dst_t, bias_t, t0, t_n):
            wr = WT.rearrange("(dc p) e -> p dc e", p=128)
            osem = OSEM[id(dst_t)]
            xsw = ([(XSEM[src][0], None), (XSEM[src][1], None)]
                   if src in XSEM else [])
            base_eV = eV.n
            rec_w = {}
            for eo in range(NE):
                rec_w[eo] = b.dma(
                    "gpsimd", sW[eo % 2],
                    out=wb_s[eo % 2].rearrange("p (dc e) -> p dc e", e=128),
                    in_=wr[:, :, eo * 128:(eo + 1) * 128],
                    waits=([(eV, base_eV + t_n * (eo - 1))] if eo >= 2
                           else ([(tM, None)] if eo == 0 else [])))
            for eo in range(NE):
                wt = wb_s[eo % 2]
                b.wait("tensor", sW[eo % 2], rec_w[eo])
                if eo == 0:
                    b.wait("tensor", vD, vD.n)
                    b.wait("tensor", sA, sA.n)
                    b.wait("tensor", eV, eV.n)
                for j in range(t_n):
                    tc = t0 + j
                    bank = ps[j % 4]
                    half = j % 2
                    xsl = xew[:, half * 1024:half * 1024 + 512]
                    rx = b.dma("sync", sX[half], out=xsl,
                               in_=XSRC[src][eo][:, tc * 512:(tc + 1) * 512],
                               waits=([(vD, None)] + xsw
                                      if (eo == 0 and j == 0) else [])
                               + [(eV, None), (osem[half], None)])
                    b.wait("tensor", eV, max(0, eV.n - (min(t_n, 4) - 1)))
                    for dc in range(NE):
                        last = (dc == NE - 1)
                        b.op("tensor", "matmul", bank[:, :],
                             wt[:, dc * 128:(dc + 1) * 128],
                             rhs_buf[:, dc * rhs_w + j * 512:
                                     dc * rhs_w + (j + 1) * 512],
                             start=(dc == 0), stop=last,
                             incs=[(tM, 1)] if last else [])
                    b.wait("vector", sX[half], rx)
                    b.op("vector", "scalar_tensor_tensor", xsl, bank[:, :],
                         bias_t[:, eo:eo + 1], xsl, ALU.add, ALU.add,
                         waits=[(tM, None)], incs=[(eV, 1)])
                    b.dma("scalar", osem[half],
                          out=dst_t[eo][:, tc * 512:(tc + 1) * 512],
                          in_=xsl, waits=[(eV, None)])

        def ffn(src, dst_t):
            with scope("ffn_ln"):
                ln_pass(src)
                normalize_all(src)
            gw = fWg.rearrange("(dc p) h -> p dc h", p=128)
            uw = fWu.rearrange("(dc p) h -> p dc h", p=128)
            fr3 = spF.rearrange("(hc p) t -> hc p t", p=128)
            base_eV = eV.n
            rec_wg = {}
            rec_wu = {}
            rec_sp = {}
            for hc in range(NH):
                wg_t = wb_s[(hc % 2) * 2]
                wu_t = wb_s[(hc % 2) * 2 + 1]
                rec_wg[hc] = b.dma(
                    "gpsimd", sW[(hc % 2) * 2],
                    out=wg_t.rearrange("p (dc e) -> p dc e", e=128),
                    in_=gw[:, :, hc * 128:(hc + 1) * 128],
                    waits=[(eV, base_eV + 8 * (hc - 1))] if hc >= 2 else
                    ([(vD, None), (tM, None)] if hc == 0 else []))
                rec_wu[hc] = b.dma(
                    "gpsimd", sW[(hc % 2) * 2 + 1],
                    out=wu_t.rearrange("p (dc e) -> p dc e", e=128),
                    in_=uw[:, :, hc * 128:(hc + 1) * 128])
            _fg = scope("ffn_gu"); _fg.__enter__()
            for hc in range(NH):
                wg_t = wb_s[(hc % 2) * 2]
                wu_t = wb_s[(hc % 2) * 2 + 1]
                ev_t = evac0 if hc % 2 == 0 else evac1
                b.wait("tensor", sW[(hc % 2) * 2], rec_wg[hc])
                b.wait("tensor", sW[(hc % 2) * 2 + 1], rec_wu[hc])
                if hc == 0:
                    b.wait("tensor", vD, vD.n)
                if hc >= 2:
                    b.wait("scalar", sSP[hc % 2], rec_sp[hc - 2])
                    b.wait("vector", sSP[hc % 2], rec_sp[hc - 2])
                for tc in range(4):
                    gbk = ps[tc % 2]
                    ubk = ps[2 + tc % 2]
                    b.wait("tensor", eV, max(0, eV.n - 2))
                    for dc in range(NE):
                        b.op("tensor", "matmul", gbk[:, :],
                             wg_t[:, dc * 128:(dc + 1) * 128],
                             xln[:, dc * 2052 + 1 + tc * 512:
                                 dc * 2052 + 1 + (tc + 1) * 512],
                             start=(dc == 0), stop=(dc == NE - 1),
                             incs=[(tM, 1)] if dc == NE - 1 else [])
                    for dc in range(NE):
                        last = (dc == NE - 1)
                        b.op("tensor", "matmul", ubk[:, :],
                             wu_t[:, dc * 128:(dc + 1) * 128],
                             xln[:, dc * 2052 + 1 + tc * 512:
                                 dc * 2052 + 1 + (tc + 1) * 512],
                             start=(dc == 0), stop=last,
                             incs=[(tM, 1)] if last else [])
                    b.op("scalar", "activation", f32t[:, :], gbk[:, :],
                         AF.Silu, waits=[(tM, None), (eV, None)],
                         incs=[(eV, 1)])
                    b.op("vector", "tensor_mul",
                         ev_t[:, tc * 512:(tc + 1) * 512], f32t[:, :],
                         ubk[:, :], waits=[(eV, None)],
                         incs=[(eV, 1)])
                rec_sp[hc] = b.dma("sync", sSP[hc % 2], out=fr3[hc],
                                   in_=ev_t[:, :], waits=[(eV, None)])
            _fg.__exit__(None, None, None)
            _fd = scope("ffn_dn"); _fd.__enter__()
            dw = fWd.rearrange("(hc p) c -> p hc c", p=128)
            fr = spF.rearrange("(hc p) t -> p hc t", p=128)
            osem = OSEM[id(dst_t)]
            xsw = ([(XSEM[src][0], None), (XSEM[src][1], None)]
                   if src in XSEM else [])
            for th in range(2):
                zld = b.dma("sync", sZH,
                            out=zhalf.rearrange("p (hc t) -> p hc t",
                                                t=1024),
                            in_=fr[:, :, th * 1024:(th + 1) * 1024],
                            waits=[(tM, None), (vD, None),
                                   (sSP[0], None), (sSP[1], None)])
                ring = [wb_s[0], wb_s[1], wb_s[2], wb_s[3], evac0, evac1]
                ring_sem = [sW[0], sW[1], sW[2], sW[3], sE[0], sE[1]]
                base_eVd = eV.n
                rec_ring = {}
                for idx in range(4 * NE):
                    eo, q4 = divmod(idx, 4)
                    w8 = []
                    if idx < 6:
                        w8 = ([(eV, None), (sSP[0], None), (sSP[1], None),
                               (tM, None)] if idx == 0 else [])
                    else:
                        prev_eo = (idx - 6) // 4
                        w8 = [(eV, base_eVd + 2 * (prev_eo + 1))]
                    rec_ring[idx] = b.dma(
                        "gpsimd", ring_sem[idx % 6],
                        out=ring[idx % 6].rearrange("p (hc e) -> p hc e",
                                                    e=128),
                        in_=dw[:, q4 * 16:(q4 + 1) * 16,
                               eo * 128:(eo + 1) * 128], waits=w8)
                for eo in range(NE):
                    for q4 in range(4):
                        b.wait("tensor", ring_sem[(4 * eo + q4) % 6],
                               rec_ring[4 * eo + q4])
                    if eo == 0:
                        b.wait("tensor", sZH, zld)
                    for j in range(2):
                        tc = th * 2 + j
                        bank = ps[4 + j]
                        xsl = xew[:, j * 1024:j * 1024 + 512]
                        rx = b.dma("sync", sX[j], out=xsl,
                                   in_=XSRC[src][eo][:, tc * 512:
                                                     (tc + 1) * 512],
                                   waits=[(eV, None), (osem[j], None)]
                                   + (xsw if (eo == 0 and j == 0) else []))
                        b.wait("tensor", eV, max(0, eV.n - 1))
                        for hc in range(NH):
                            last = (hc == NH - 1)
                            b.op("tensor", "matmul", bank[:, :],
                                 ring[(4 * eo + hc // 16) % 6][
                                     :, (hc % 16) * 128:(hc % 16) * 128 + 128],
                                 zhalf[:, hc * 1024 + j * 512:
                                       hc * 1024 + (j + 1) * 512],
                                 start=(hc == 0), stop=last,
                                 incs=[(tM, 1)] if last else [])
                        b.wait("vector", sX[j], rx)
                        b.op("vector", "tensor_add", xsl, xsl, bank[:, :],
                             waits=[(tM, None)], incs=[(eV, 1)])
                        b.dma("scalar", osem[j],
                              out=dst_t[eo][:, tc * 512:(tc + 1) * 512],
                              in_=xsl, waits=[(eV, None)])

        import contextlib
        scope = nc.named_scope
        if dbg == "xcopy":
            _dump_hbm(X_IN)
            return
        with scope("ln1"):
            ln_pass(0)
        if dbg == "mub":
            _dump_sb(mub)
            return
        if dbg == "rstdb":
            _dump_sb(rstdb)
            return
        with scope("norm1"):
            normalize_all(0)
        if dbg == "xln0":
            _dump_xln()
            return
        for p, mnm, act, spill in (("k", "maa_k", AF.Identity, spK),
                                   ("r", "maa_r", AF.Identity, spR),
                                   ("v", "maa_v", AF.Identity, spV),
                                   ("g", "maa_g", AF.Silu, spG)):
            with scope("proj_a_" + p):
                mix_all(prm[mnm], "a")
                proj_fm(aW[p], bias[p], act, spill)
        if dbg in ("xln_a", "k_a", "g_a"):
            srcs = {"xln_a": None, "k_a": spK, "g_a": spG}
            if dbg == "xln_a":
                _dump_xln()
            else:
                _dump_hbm(srcs[dbg])
            return

        _ms = scope("maskbuild"); _ms.__enter__()
        for h in range(NHEAD):
            for half in range(2):
                dst = masks[:, h * 512 + half * 256:h * 512 + half * 256 + 256]
                b.op("scalar", "activation", f32t[:, 0:256],
                     dtile[:, half * 256:half * 256 + 256], AF.Exp,
                     scale=lnw_b[:, h:h + 1], waits=[(vD, None)],
                     incs=[(sA, 1)])
                b.op("vector", "tensor_mul", f32t[:, 256:512],
                     f32t[:, 0:256], stepd[:, half * 256:half * 256 + 256],
                     waits=[(sA, None)], incs=[(vD, 1)])
                b.op("vector", "scalar_tensor_tensor", dst,
                     diagd[:, half * 256:half * 256 + 256], u_b[:, h:h + 1],
                     f32t[:, 256:512], ALU.mult, ALU.add, incs=[(vD, 1)])
        for j in range(NE):
            for hp in range(2):
                h = 2 * j + hp
                b.op("scalar", "activation",
                     wbpk[hp * 64:hp * 64 + 64, j * 256:(j + 1) * 256],
                     iota_b[hp * 64:hp * 64 + 64, :], AF.Exp,
                     scale=lnw_b[hp * 64:hp * 64 + 64, h:h + 1],
                     incs=[(sA, 1)])

        if dbg == "mask":
            dr = DBG.rearrange("(c p) t -> c p t", p=128)
            for i in range(8):
                b.wait("vector", vD, vD.n)
                b.wait("vector", sA, sA.n)
                b.wait("vector", sDB, sDB.n)
                b.op("vector", "tensor_copy", xew[:, :],
                     masks[:, i * 2048:(i + 1) * 2048], incs=[(vD, 1)])
                b.dma("sync", sDB, out=dr[i], in_=xew[:, :],
                      waits=[(vD, None)])
            for i in range(8, 16):
                b.dma("sync", sDB, out=dr[i], in_=xew[:, :],
                      waits=[(vD, None), (sDB, None)])
            _zero_out()
            return
        kr3 = spK.rearrange("(ec p) t -> p ec t", p=128)
        rr3 = spR.rearrange("(ec p) t -> p ec t", p=128)
        gr3 = spG.rearrange("(ec p) t -> p ec t", p=128)
        zr3 = spZ.rearrange("(ec p) t -> p ec t", p=128)
        b.op("vector", "memset", stt_f[:, :], 0.0, incs=[(vD, 1)])
        b.op("vector", "memset", stt_b[:, :], 0.0, incs=[(vD, 1)])

        rec_kb = {}
        rec_rb = {}
        rec_kt = {}
        rec_vt = {}
        rec_gb = {}

        def kr_loads(c):
            sl = slice(c * 256, (c + 1) * 256)
            rec_kb[c] = b.dma(
                "sync", sKB[c % 2],
                out=kb_s[c % 2].rearrange("p (ec t) -> p ec t", t=256),
                in_=kr3[:, :, sl],
                waits=[(tM, None), (vD, None)] if c >= 2 else
                ([(sSP[0], None), (sSP[1], None)] if c == 0 else []))
            rec_rb[c] = b.dma(
                "sync", sRB[c % 2],
                out=rb_s[c % 2].rearrange("p (ec t) -> p ec t", t=256),
                in_=rr3[:, :, sl])
            for half in range(2):
                rec_kt[c] = b.dma(
                    "sync", sKT[c % 2],
                    out=ktm_s[c % 2][:, half * 2048:(half + 1) * 2048],
                    in_=spK[:, c * 256 + half * 128:c * 256 +
                            (half + 1) * 128], transpose=True)

        def v_loads(c, wait_tm):
            for half in range(2):
                w9 = [(tM, None)] if (wait_tm and half == 0) else []
                if c == 0 and half == 0:
                    w9 = [(sSP[0], None), (sSP[1], None)]
                rec_vt[c] = b.dma(
                    "scalar", sVT[c % 2],
                    out=vtm_s[c % 2][:, half * 2048:(half + 1) * 2048],
                    in_=spV[:, c * 256 + half * 128:c * 256 +
                            (half + 1) * 128], transpose=True, waits=w9)

        def g_loads(c, wait_vd):
            sl = slice(c * 256, (c + 1) * 256)
            w9 = [(vD, None)] if wait_vd else []
            if c == 0:
                w9 = [(sSP[0], None), (sSP[1], None)]
            rec_gb[c] = b.dma(
                "scalar", sGB,
                out=gb.rearrange("p (ec t) -> p ec t", t=256),
                in_=gr3[:, :, sl], waits=w9)

        _at = scope("attn_a"); _at.__enter__()
        kr_loads(0)
        v_loads(0, False)
        g_loads(0, False)
        for c in range(NCH):
            kb_t = kb_s[c % 2]
            rb_t = rb_s[c % 2]
            kt_t = ktm_s[c % 2]
            vtm = vtm_s[c % 2]
            if c + 1 < NCH:
                kr_loads(c + 1)
            if c > 0:
                b.op("vector", "tensor_copy", stt_b[:, :], stt_f[:, :],
                     waits=[(tM, None)], incs=[(vD, 1)])
            base_eA, base_eY, base_eS = eA.n, eY.n, eS.n
            for h in range(NHEAD):
                pb = h % 2
                hb = pb * 64
                ec = h // 2
                a_bank = ps[pb]
                y_bank = ps[2 + pb]
                s_bank = ps[4 + pb]
                ksl = kb_t[hb:hb + 64, ec * 256:(ec + 1) * 256]
                rsl = rb_t[hb:hb + 64, ec * 256:(ec + 1) * 256]
                if h == 0:
                    b.wait("tensor", sKB[c % 2], rec_kb[c])
                    b.wait("tensor", sRB[c % 2], rec_rb[c])
                    b.wait("tensor", sKT[c % 2], rec_kt[c])
                    b.wait("tensor", sVT[c % 2], rec_vt[c])
                    b.wait("tensor", vD, vD.n)
                    b.wait("tensor", eV, eV.n)
                b.wait("tensor", eA, max(0, base_eA + h - 1))
                for half in range(2):
                    b.op("tensor", "matmul",
                         a_bank[:, half * 256:half * 256 + 256],
                         ksl[:, half * 128:half * 128 + 128], rsl,
                         start=True, stop=True,
                         incs=[(tM, 1)] if half == 1 else [])
                kt_half = ktil[:, pb * 128:pb * 128 + 128]
                b.op("vector", "tensor_scalar_mul", kt_half[:, 0:64],
                     kt_t[:, h * 64:h * 64 + 64], wkv_c0[:, h:h + 1],
                     waits=[(tM, None)] + ([(sKT[c % 2], rec_kt[c])]
                                           if h == 0 else []),
                     incs=[(vD, 1)])
                b.op("vector", "tensor_scalar_mul", kt_half[:, 64:128],
                     kt_t[:, 2048 + h * 64:2048 + h * 64 + 64],
                     wkv_c1[:, h:h + 1], incs=[(vD, 1)])
                am_t = attm[:, pb * 512:pb * 512 + 512]
                b.op("vector", "tensor_mul", am_t[:, 0:256],
                     a_bank[:, 0:256], masks[:, h * 512:h * 512 + 256],
                     waits=[(tM, None)], incs=[(vD, 1)])
                b.op("vector", "tensor_mul", am_t[:, 256:512],
                     a_bank[:, 256:512],
                     masks[:, h * 512 + 256:h * 512 + 512],
                     incs=[(eA, 1)])
                if dbg == "attm0" and c == 0 and h == 0:
                    dmp0 = True
                    dr = DBG.rearrange("(c p) t -> c p t", p=128)
                    b.op("vector", "tensor_copy", xew[:, 0:512],
                         a_bank[:, :], waits=[(tM, None)], incs=[(vD, 1)])
                    b.op("vector", "tensor_copy", xew[:, 512:1024],
                         am_t[:, :], incs=[(vD, 1)])
                    b.op("vector", "tensor_copy", xew[0:64, 1024:1280],
                         ksl, incs=[(vD, 1)])
                    b.op("vector", "tensor_copy", xew[0:64, 1280:1536],
                         rsl, incs=[(vD, 1)])
                    b.op("vector", "tensor_copy", xew[:, 1536:1792],
                         vtm[:, 0:256], incs=[(vD, 1)])
                    b.wait("tensor", eA, eA.n)
                    for half in range(2):
                        b.op("tensor", "matmul", y_bank[0:64, 0:256],
                             vtm[:, half * 2048:half * 2048 + 64],
                             am_t[:, half * 256:half * 256 + 256],
                             start=(half == 0), stop=False)
                    b.op("tensor", "matmul", y_bank[0:64, 0:256],
                         stt_b[0:64, 0:64], rtil[0:64, :],
                         start=False, stop=True, incs=[(tM, 1)])
                    b.op("vector", "tensor_copy", xew[0:64, 1792:2048],
                         y_bank[0:64, 0:256], waits=[(tM, None)],
                         incs=[(vD, 1)])
                    b.op("vector", "tensor_copy", xew[0:64, 1536:1600],
                         stt_b[0:64, 0:64], incs=[(vD, 1)])
                    b.op("vector", "tensor_copy", xew[0:64, 1600:1664],
                         stt_f[0:64, 0:64], incs=[(vD, 1)])
                    b.op("vector", "tensor_copy", xew[0:64, 1664:1728],
                         wbpk[0:64, 4032:4096], incs=[(vD, 1)])
                    for i in range(16):
                        b.dma("sync", sDB, out=dr[i], in_=xew[:, :],
                              waits=[(vD, None), (sDB, None)])
                    _zero_out()
                    return
                b.op("vector", "tensor_mul", rtil[hb:hb + 64, :], rsl,
                     wbpk[hb:hb + 64, ec * 256:(ec + 1) * 256],
                     waits=[(tM, None)] + ([(sRB[c % 2], rec_rb[c])]
                                           if h == 0 else []),
                     incs=[(vD, 1)])
                b.wait("tensor", vD, vD.n)
                cb = (h // 2) * 64
                b.wait("tensor", eS, max(0, base_eS + h - 1))
                for half in range(2):
                    b.op("tensor", "matmul", s_bank[hb:hb + 64, 0:64],
                         kt_half[:, half * 64:half * 64 + 64],
                         vtm[:, half * 2048 + h * 64:
                             half * 2048 + h * 64 + 64],
                         start=(half == 0), stop=(half == 1),
                         incs=[(tM, 1)] if half == 1 else [])
                b.wait("tensor", eY, max(0, base_eY + h - 1))
                b.wait("tensor", eA, eA.n)
                for half in range(2):
                    b.op("tensor", "matmul", y_bank[hb:hb + 64, 0:256],
                         vtm[:, half * 2048 + h * 64:
                             half * 2048 + h * 64 + 64],
                         am_t[:, half * 256:half * 256 + 256],
                         start=(half == 0), stop=False)
                b.op("tensor", "matmul", y_bank[hb:hb + 64, 0:256],
                     stt_b[hb:hb + 64, cb:cb + 64], rtil[hb:hb + 64, :],
                     start=False, stop=True, incs=[(tM, 1)])
                b.op("vector", "scalar_tensor_tensor",
                     stt_f[hb:hb + 64, cb:cb + 64],
                     stt_f[hb:hb + 64, cb:cb + 64], ws_b[hb:hb + 64, h:h + 1],
                     s_bank[hb:hb + 64, 0:64], ALU.mult, ALU.add,
                     waits=[(tM, None)], incs=[(eS, 1)])
                ydst = (evac0 if ec < 8 else evac1)[:, (ec % 8) * 256:
                                                    (ec % 8) * 256 + 256]
                b.op("scalar", "activation", ydst[hb:hb + 64, :],
                     y_bank[hb:hb + 64, 0:256], AF.Identity,
                     waits=[(tM, None)], incs=[(eY, 1)])
            if c + 1 < NCH:
                v_loads(c + 1, c + 1 >= 2)
            b.wait("tensor", eY, eY.n)
            b.wait("tensor", eS, eS.n)
            b.wait("tensor", vD, vD.n)
            for ec in range(NE):
                ysrc = (evac0 if ec < 8 else evac1)[:, (ec % 8) * 256:
                                                    (ec % 8) * 256 + 256]
                b.op("tensor", "matmul", ps[6][0:32, 0:256],
                     bdiag[:, ec * 32:(ec + 1) * 32], ysrc,
                     start=(ec == 0), stop=(ec == NE - 1))
                sq_t = sqh[:, (ec % 2) * 256:(ec % 2) * 256 + 256]
                b.op("scalar", "activation", sq_t, ysrc, AF.Square,
                     waits=[(tM, max(0, tM.n - 1))], incs=[(sA, 1)])
                b.wait("tensor", sA, sA.n)
                b.op("tensor", "matmul", ps[7][0:32, 0:256],
                     bdiag[:, ec * 32:(ec + 1) * 32], sq_t,
                     start=(ec == 0), stop=(ec == NE - 1),
                     incs=[(tM, 1)])
            g_mu = gnst[:, 0:256]
            g_rs = gnst[:, 256:512]
            g_t1 = gnst[:, 512:768]
            b.op("vector", "tensor_scalar_mul", g_mu, ps[6][0:32, 0:256],
                 1.0 / HN, waits=[(tM, None)], incs=[(vD, 1)])
            b.op("vector", "tensor_scalar_mul", g_rs, ps[7][0:32, 0:256],
                 1.0 / HN, incs=[(vD, 1)])
            b.op("vector", "tensor_mul", g_t1, g_mu, g_mu, incs=[(vD, 1)])
            b.op("vector", "tensor_sub", g_rs, g_rs, g_t1, incs=[(vD, 1)])
            b.op("scalar", "activation", g_rs, g_rs, AF.Sqrt,
                 bias=epsgn[0:32, :], waits=[(vD, None)], incs=[(sA, 1)])
            b.op("vector", "reciprocal", g_rs, g_rs, waits=[(sA, None)],
                 incs=[(vD, 1)])
            b.op("vector", "tensor_mul", g_t1, g_mu, g_rs, incs=[(vD, 1)])
            b.op("vector", "tensor_copy", gnrow_a[:, :], g_rs,
                 incs=[(vD, 1)])
            b.op("vector", "tensor_copy", gnrow_m[:, :], g_t1,
                 incs=[(vD, 1)])
            for ec in range(NE):
                ysrc = (evac0 if ec < 8 else evac1)[:, (ec % 8) * 256:
                                                    (ec % 8) * 256 + 256]
                ba_b = ps[6] if ec % 2 == 0 else ps[4]
                bm_b = ps[7] if ec % 2 == 0 else ps[5]
                b.wait("tensor", vD, vD.n)
                b.wait("tensor", eV, max(0, eV.n - 2))
                b.op("tensor", "matmul", ba_b[0:128, 0:256],
                     sel[:, ec * 128:(ec + 1) * 128], gnrow_a[:, :],
                     start=True, stop=True)
                b.op("tensor", "matmul", bm_b[0:128, 0:256],
                     sel[:, ec * 128:(ec + 1) * 128], gnrow_m[:, :],
                     start=True, stop=True, incs=[(tM, 1)])
                b.op("vector", "tensor_mul", f32t[:, 0:256], ysrc,
                     ba_b[0:128, 0:256], waits=[(tM, None)], incs=[(vD, 1)])
                b.op("vector", "tensor_sub", f32t[:, 0:256], f32t[:, 0:256],
                     bm_b[0:128, 0:256], incs=[(eV, 2)])
                if dbg != "y_a":
                    b.op("vector", "tensor_scalar", f32t[:, 256:512],
                         f32t[:, 0:256], prm["alng"][:, ec:ec + 1],
                         prm["alnb"][:, ec:ec + 1], ALU.mult, ALU.add,
                         incs=[(vD, 1)])
                    if ec == 0:
                        b.wait("vector", sGB, rec_gb[c])
                    b.op("vector", "tensor_mul", ysrc, f32t[:, 256:512],
                         gb[:, ec * 256:(ec + 1) * 256], incs=[(vD, 1)])
            if c + 1 < NCH:
                g_loads(c + 1, True)
            for half, ev_t in ((0, evac0), (1, evac1)):
                b.dma("sync", sSP[half],
                      out=zr3[:, half * 8:half * 8 + 8,
                              c * 256:(c + 1) * 256],
                      in_=ev_t.rearrange("p (ec t) -> p ec t", t=256),
                      waits=[(vD, None)])
            b.wait("scalar", sSP[0], sSP[0].n)
            b.wait("scalar", sSP[1], sSP[1].n)
        _at.__exit__(None, None, None)
        if dbg in ("z_a", "y_a"):
            _dump_hbm(spZ)
            return

        if dbg == "zhalf":
            zr0 = spZ.rearrange("(ec p) t -> p ec t", p=128)
            zl0 = b.dma("sync", sZH,
                        out=zhalf[:, 0:16 * 1024].rearrange(
                            "p (ec t) -> p ec t", t=1024),
                        in_=zr0[:, :, 0:1024],
                        waits=[(tM, None), (vD, None),
                               (sSP[0], None), (sSP[1], None)])
            dr = DBG.rearrange("(c p) t -> c p t", p=128)
            for i in range(8):
                b.wait("vector", vD, vD.n)
                b.wait("vector", sDB, sDB.n)
                b.wait("vector", sZH, zl0)
                b.op("vector", "tensor_copy", xew[:, :],
                     zhalf[:, i * 2048:(i + 1) * 2048], incs=[(vD, 1)])
                b.dma("sync", sDB, out=dr[i], in_=xew[:, :],
                      waits=[(vD, None)])
            for i in range(8, 16):
                b.dma("sync", sDB, out=dr[i], in_=xew[:, :],
                      waits=[(vD, None), (sDB, None)])
            _zero_out()
            return
        zr = spZ.rearrange("(ec p) t -> p ec t", p=128)
        with scope("wo_a"):
            for th in range(2):
                zl = b.dma("sync", sZH,
                           out=zhalf[:, 0:16 * 1024].rearrange(
                               "p (ec t) -> p ec t", t=1024),
                           in_=zr[:, :, th * 1024:(th + 1) * 1024],
                           waits=[(tM, None), (vD, None),
                                  (sSP[0], None), (sSP[1], None)])
                b.wait("tensor", sZH, zl)
                residual_proj(aW["o"], zhalf, 1024, 0, xr_B, bias["o"],
                              th * 2, 2)
        if n_sub == 1 or dbg == "x1":
            _final_copy(xr_B)
            return

        ffn(2, xr_A if n_sub > 2 else xr_O)
        if n_sub == 2:
            return

        if dbg == "x2in3":
            _dump_hbm(xA)
            return
        with scope("ln3"):
            ln_pass(1)
            normalize_all(1)
        for p, mnm, act, spill, tmax in (("k", "tm_k", AF.Identity, spK, True),
                                         ("v", "tm_v", AF.Identity, spV,
                                          False),
                                         ("r", "tm_r", AF.Sigmoid, spR,
                                          False)):
            with scope("proj_b_" + p):
                mix_all(prm[mnm], "b")
                proj_fm(bW[p], zbias, act, spill, track_max=tmax)
        b.op("vector", "tensor_scalar_mul", negM[:, :], Mmax[:, :], -1.0,
             incs=[(vD, 1)])
        if dbg == "k_b":
            _dump_hbm(spK)
            return
        kr = spK.rearrange("(ec p) t -> ec p t", p=128)
        vr = spV.rearrange("(ec p) t -> ec p t", p=128)
        rr = spR.rearrange("(ec p) t -> ec p t", p=128)
        wE = w4[:, 0:2048]
        wEV = w4[:, 2048:4096]
        wA = w4[:, 4096:6144]
        wB = w4[:, 6144:8192]
        _wk = scope("wkv"); _wk.__enter__()
        b.op("vector", "memset", xew[:, 0:2048], 1.0, incs=[(vD, 1)])
        for ec in range(NE):
            rk = b.dma("sync", sE[0], out=evac0[:, :], in_=kr[ec],
                       waits=[(sA, None)] +
                       ([(sSP[0], None), (sSP[1], None)] if ec == 0 else []))
            rv = b.dma("sync", sE[1], out=evac1[:, :], in_=vr[ec],
                       waits=[(vD, None)])
            b.wait("scalar", sE[0], rk)
            b.wait("vector", sE[1], rv)
            b.op("scalar", "activation", wE, evac0[:, :], AF.Exp,
                 bias=negM[:, ec:ec + 1], waits=[(vD, None)], incs=[(sA, 1)])
            b.op("vector", "tensor_mul", wEV, wE, evac1[:, :],
                 waits=[(sA, None)], incs=[(vD, 1)])
            b.op("vector", "tensor_tensor_scan", wA, xew[:, 0:2048], wEV,
                 0.0, ALU.mult, ALU.add, incs=[(vD, 1)])
            b.op("vector", "tensor_tensor_scan", wB, xew[:, 0:2048], wE,
                 0.0, ALU.mult, ALU.add, incs=[(vD, 1)])
            b.op("vector", "scalar_tensor_tensor", wA, wEV,
                 gu[:, ec:ec + 1], wA, ALU.mult, ALU.add, incs=[(vD, 1)])
            b.op("vector", "scalar_tensor_tensor", wB, wE,
                 gu[:, ec:ec + 1], wB, ALU.mult, ALU.add, incs=[(vD, 1)])
            b.op("vector", "reciprocal", wB, wB, incs=[(vD, 1)])
            b.op("vector", "tensor_mul", wkvb[:, ec * 2048:(ec + 1) * 2048],
                 wA, wB, incs=[(vD, 1)])
            b.wait("tensor", vD, vD.n)
            for tc in range(4):
                b.op("tensor", "matmul", ps[6][0:4, :],
                     oneh4[:, 4 * tc:4 * tc + 4],
                     wkvb[:, ec * 2048 + tc * 512:ec * 2048 +
                          (tc + 1) * 512],
                     start=(ec == 0 and tc == 0), stop=False)
            for tc in range(4):
                sq_t = sqh[:, (tc % 2) * 512:(tc % 2) * 512 + 512]
                b.op("scalar", "activation", sq_t,
                     wkvb[:, ec * 2048 + tc * 512:ec * 2048 +
                          (tc + 1) * 512], AF.Square,
                     waits=[(vD, None) if tc == 0 else (tM, None)],
                     incs=[(sA, 1)])
                b.wait("tensor", sA, sA.n)
                b.op("tensor", "matmul", ps[7][0:4, :],
                     oneh4[:, 4 * tc:4 * tc + 4], sq_t,
                     start=(ec == 0 and tc == 0),
                     stop=(ec == NE - 1 and tc == 3),
                     incs=[(tM, 1)])
        _wk.__exit__(None, None, None)
        if dbg == "wkv":
            dr = DBG.rearrange("(c p) t -> c p t", p=128)
            for ec in range(NE):
                b.wait("vector", vD, vD.n)
                b.wait("vector", sDB, sDB.n)
                b.op("vector", "tensor_copy", xew[:, :],
                     wkvb[:, ec * 2048:(ec + 1) * 2048], incs=[(vD, 1)])
                b.dma("sync", sDB, out=dr[ec], in_=xew[:, :],
                      waits=[(vD, None)])
            _zero_out()
            return
        with scope("lnb_fin"):
            ln_finalize()
        if dbg == "mub3":
            _dump_sb(mub)
            return
        if dbg == "rstdb3":
            _dump_sb(rstdb)
            return
        _pb = scope("passB"); _pb.__enter__()
        for ec in range(NE):
            rrc = b.dma("sync", sE[0], out=evac0[:, :], in_=rr[ec],
                        waits=[(vD, None)] +
                        ([(sSP[0], None), (sSP[1], None)] if ec == 0 else []))
            b.op("vector", "tensor_sub", xew[:, 0:2048],
                 wkvb[:, ec * 2048:(ec + 1) * 2048], mub[:, :],
                 incs=[(vD, 1)])
            b.op("vector", "tensor_mul", xew[:, 0:2048], xew[:, 0:2048],
                 rstdb[:, :], incs=[(vD, 1)])
            b.op("vector", "tensor_scalar", xew[:, 0:2048], xew[:, 0:2048],
                 prm["blng"][:, ec:ec + 1], prm["blnb"][:, ec:ec + 1],
                 ALU.mult, ALU.add, incs=[(vD, 1)])
            b.wait("vector", sE[0], rrc)
            b.op("vector", "tensor_mul", xmix[:, ec * 2048:(ec + 1) * 2048],
                 xew[:, 0:2048], evac0[:, :], incs=[(vD, 1)])
        _pb.__exit__(None, None, None)
        if dbg == "rwkv":
            dr = DBG.rearrange("(c p) t -> c p t", p=128)
            for ec in range(NE):
                b.wait("vector", vD, vD.n)
                b.wait("vector", sDB, sDB.n)
                b.op("vector", "tensor_copy", xew[:, :],
                     xmix[:, ec * 2048:(ec + 1) * 2048], incs=[(vD, 1)])
                b.dma("sync", sDB, out=dr[ec], in_=xew[:, :],
                      waits=[(vD, None)])
            _zero_out()
            return
        with scope("wo_b"):
            residual_proj(bW["o"], xmix, 2048, 1,
                          xr_B if n_sub > 3 else xr_O, zbias, 0, 4)
        if n_sub == 3:
            return

        ffn(2, xr_O)

    def _final_copy(src_t):
        sems = OSEM.get(id(src_t))
        for ec in range(NE):
            r1 = b.dma("sync", sDB, out=xew[:, :], in_=src_t[ec],
                       waits=[(vD, None), (sDB, None)] +
                       ([(sems[0], None), (sems[1], None)] if sems else []))
            b.wait("sync", sDB, r1)
            b.dma("sync", sXO[ec % 2], out=xr_O[ec], in_=xew[:, :])

    def _dump_sb(buf):
        dr = DBG.rearrange("(c p) t -> c p t", p=128)
        for ec in range(NE):
            b.wait("vector", vD, vD.n)
            b.wait("vector", sDB, sDB.n)
            b.op("vector", "tensor_copy", xew[:, :], buf[:, :], incs=[(vD, 1)])
            b.dma("sync", sDB, out=dr[ec], in_=xew[:, :],
                  waits=[(vD, None)])
        _zero_out()

    def _dump_xln():
        dr = DBG.rearrange("(c p) t -> c p t", p=128)
        for ec in range(NE):
            b.wait("vector", vD, vD.n)
            b.wait("vector", sDB, sDB.n)
            b.op("vector", "tensor_copy", xew[:, :],
                 xln[:, ec * 2052 + 1:ec * 2052 + 2049], incs=[(vD, 1)])
            b.dma("sync", sDB, out=dr[ec], in_=xew[:, :],
                  waits=[(vD, None)])
        _zero_out()

    def _dump_hbm(src):
        sr = src.rearrange("(c p) t -> c p t", p=128)
        dr = DBG.rearrange("(c p) t -> c p t", p=128)
        isf32 = src.dtype == F32
        for ec in range(NE):
            tmp = xew if isf32 else evac0
            r1 = b.dma("sync", sDB, out=tmp[:, :], in_=sr[ec],
                       waits=[(vD, None), (sA, None), (sDB, None),
                              (sSP[0], None), (sSP[1], None),
                              (sXA[0], None), (sXA[1], None),
                              (sXB[0], None), (sXB[1], None)])
            if not isf32:
                b.wait("vector", sDB, r1)
                b.op("vector", "tensor_copy", xew[:, :], evac0[:, :],
                     incs=[(vD, 1)])
            b.dma("sync", sDB, out=dr[ec], in_=xew[:, :],
                  waits=[(vD, None), (sDB, None)])
        _zero_out()

    def _zero_out():
        b.wait("vector", sDB, sDB.n)
        b.op("vector", "memset", xew[:, :], 0.0, incs=[(vD, 1)])
        for ec in range(NE):
            b.dma("sync", sXO[ec % 2], out=xr_O[ec], in_=xew[:, :],
                  waits=[(vD, None)])

    with nc.Block() as blk:
        @blk.sync
        def _(eng):
            b.begin("sync", eng)
            emit_all()
            for s in b.sems:
                if s.name[0] == "s" and s.n > 0:
                    eng.wait_ge(s.h, s.n)

        @blk.scalar
        def _(eng):
            b.begin("scalar", eng)
            emit_all()

        @blk.vector
        def _(eng):
            b.begin("vector", eng)
            emit_all()

        @blk.tensor
        def _(eng):
            b.begin("tensor", eng)
            emit_all()

        @blk.gpsimd
        def _(eng):
            b.begin("gpsimd", eng)
            emit_all()

    return nc


def _host_constants():
    p = np.arange(128)
    c = {}
    c["c_ones"] = np.ones((1, 512), np.float32)
    c["c_onesb"] = np.ones((1, 512), ml_dtypes.bfloat16)
    c["c_iota"] = np.arange(256, dtype=np.float32)[None, :]
    c["c_revi"] = (255.0 - np.arange(256, dtype=np.float32))[None, :]
    c["c_iotab"] = np.broadcast_to(np.arange(256, dtype=np.float32),
                                   (128, 256)).copy()
    dt = np.zeros((128, 512), np.float32)
    st = np.zeros((128, 512), np.float32)
    dg = np.zeros((128, 512), np.float32)
    for half in range(2):
        ti = np.arange(256)[None, :]
        tj = (p + 128 * half)[:, None]
        Dm = ti - tj
        dt[:, half * 256:(half + 1) * 256] = np.where(Dm > 0, Dm - 1.0, 0.0)
        st[:, half * 256:(half + 1) * 256] = (Dm > 0)
        dg[:, half * 256:(half + 1) * 256] = (Dm == 0)
    c["c_dtile"] = dt
    c["c_step"] = st.astype(ml_dtypes.bfloat16)
    c["c_diag"] = dg.astype(ml_dtypes.bfloat16)
    bd = np.zeros((128, 512), np.float32)
    for ec in range(16):
        bd[0:64, ec * 32 + 2 * ec] = 1.0
        bd[64:128, ec * 32 + 2 * ec + 1] = 1.0
    c["c_bdiag"] = bd.astype(ml_dtypes.bfloat16)
    se = np.zeros((32, 2048), np.float32)
    for ec in range(16):
        for pp in range(128):
            se[2 * ec + pp // 64, ec * 128 + pp] = 1.0
    c["c_sel"] = se.astype(ml_dtypes.bfloat16)
    o4 = np.zeros((128, 16), np.float32)
    for a in range(4):
        o4[:, 4 * a + a] = 1.0
    c["c_oneh4"] = o4.astype(ml_dtypes.bfloat16)
    return c


def _pp(v):
    return np.ascontiguousarray(np.asarray(v, np.float32).reshape(16, 128).T)


_NC_CACHE = {}


def _base_inputs(g):
    f32 = np.float32
    wT = lambda w: np.ascontiguousarray(np.asarray(w, f32).T)
    base = {
        "aWk": wT(g["L0_attn_a_Wk"]), "aWr": wT(g["L0_attn_a_Wr"]),
        "aWv": wT(g["L0_attn_a_Wv"]), "aWg": wT(g["L0_attn_a_Wg"]),
        "aWo": wT(g["L0_attn_a_Wo"]),
        "abk": _pp(g["L0_attn_a_bk"]), "abr": _pp(g["L0_attn_a_br"]),
        "abv": _pp(g["L0_attn_a_bv"]), "abg": _pp(g["L0_attn_a_bg"]),
        "abo": _pp(g["L0_attn_a_bo"]),
        "bWk": wT(g["L0_attn_b_Wk"]), "bWv": wT(g["L0_attn_b_Wv"]),
        "bWr": wT(g["L0_attn_b_Wr"]), "bWo": wT(g["L0_attn_b_Wo"]),
        "fWg": wT(g["L0_ffn_w_gate"]), "fWu": wT(g["L0_ffn_w_up"]),
        "fWd": wT(g["L0_ffn_w_down"]),
        "maa_k": _pp(g["L0_attn_a_maa_k"]),
        "maa_v": _pp(g["L0_attn_a_maa_v"]),
        "maa_r": _pp(g["L0_attn_a_maa_r"]),
        "maa_g": _pp(g["L0_attn_a_maa_g"]),
        "alng": _pp(g["L0_attn_a_ln_g"]), "alnb": _pp(g["L0_attn_a_ln_b"]),
        "tm_k": _pp(np.asarray(g["L0_attn_b_tm_k"]).reshape(-1)),
        "tm_v": _pp(np.asarray(g["L0_attn_b_tm_v"]).reshape(-1)),
        "tm_r": _pp(np.asarray(g["L0_attn_b_tm_r"]).reshape(-1)),
        "blng": _pp(g["L0_attn_b_ln_g"]), "blnb": _pp(g["L0_attn_b_ln_b"]),
        "btf": _pp(g["L0_attn_b_time_first"]),
        "atd_row": np.ascontiguousarray(
            np.asarray(g["L0_attn_a_time_decay"], f32)[None, :]),
        "atf_row": np.ascontiguousarray(
            np.asarray(g["L0_attn_a_time_faaaa"], f32)[None, :]),
    }
    base.update(_host_constants())
    return base


def kernel(**inputs):
    f32 = np.float32
    g = {k: np.asarray(v) for k, v in inputs.items()}
    _ = g["L0_attn_b_time_decay"]
    x = np.asarray(g["L0_attn_a_x"], f32)
    ncores = 8
    if "full" not in _NC_CACHE:
        _NC_CACHE["full"] = build_nc()
    nc = _NC_CACHE["full"]
    base = _base_inputs(g)
    in_maps = []
    for i in range(ncores):
        m = dict(base)
        m["x_in"] = np.ascontiguousarray(x[i].T)
        in_maps.append(m)
    res = run_bass_kernel_spmd(nc, in_maps, core_ids=list(range(ncores)))
    out = np.stack([np.asarray(r["y_out"]).T for r in res.results])
    return np.ascontiguousarray(out.astype(f32))
